# version 02 (final)
# baseline (speedup 1.0000x reference)
import numpy as np
from contextlib import ExitStack
import concourse.bass as bass
import concourse.mybir as mybir
from concourse.bass_utils import run_bass_kernel_spmd

F32 = mybir.dt.float32
I32 = mybir.dt.int32
AF = mybir.ActivationFunctionType
OPT = mybir.AluOpType
AX = mybir.AxisListType.X

NCORES = 8
B, T, D, H, DK, DV = 2, 16384, 1024, 16, 64, 64
PT = 4096
SLAB = 512
NSLAB = PT // SLAB
EPS = 1e-6
LNEPS = 1e-5


class Prog:
    def __init__(self, nc):
        self.nc = nc
        self.items = []

    def t(self, f): self.items.append(("tensor", f, 1))
    def v(self, f): self.items.append(("vector", f, 1))
    def s(self, f): self.items.append(("scalar", f, 1))
    def d(self, f): self.items.append(("sync", f, 16))

    def mm(self, out, lhsT, rhs, st=True, sp=True):
        self.t(lambda e, o=out, l=lhsT, r=rhs, a=st, b=sp: e.matmul(o, l, r, start=a, stop=b))

    def tr(self, out, in_, ident):
        self.t(lambda e, o=out, i=in_, d_=ident: e.transpose(o, i, d_))

    def act(self, out, in_, fn):
        self.s(lambda e, o=out, i=in_, f=fn: e.activation(o, i, f))

    def cp(self, out, in_):
        self.v(lambda e, o=out, i=in_: e.tensor_copy(o, i))

    def tt(self, out, a, b, op):
        self.v(lambda e, o=out, x=a, y=b, p=op: e.tensor_tensor(o, x, y, p))

    def ts(self, out, in_, s1, op, s2=None, op2=OPT.bypass):
        if s2 is None:
            self.v(lambda e, o=out, i=in_, a=s1, p=op: e.tensor_single_scalar(o, i, a, p))
        else:
            self.v(lambda e, o=out, i=in_, a=s1, b=s2, p=op, q=op2: e.tensor_scalar(o, i, a, b, p, q))

    def red(self, out, in_, op, absv=False):
        self.v(lambda e, o=out, i=in_, p=op, a=absv: e.tensor_reduce(o, i, AX, p, apply_absolute_value=a))

    def recip(self, out, in_):
        self.v(lambda e, o=out, i=in_: e.reciprocal(o, i))

    def dma(self, out, in_):
        self.d(lambda e, o=out, i=in_: e.dma_start(out=o, in_=i))

    def dmaT(self, out, in_):
        self.d(lambda e, o=out, i=in_: e.dma_start_transpose(out=o, in_=i))

    def seal(self):
        nc = self.nc
        with nc.semaphore("g") as g, nc.Block() as blk:
            def mk(name):
                def emit(eng):
                    n = 0
                    for (e_, f, inc) in self.items:
                        if e_ == name:
                            if n:
                                eng.wait_ge(g, n)
                            f(eng).then_inc(g, inc)
                        n += inc
                    if name == "sync":
                        eng.wait_ge(g, n)
                return emit
            blk.tensor(mk("tensor"))
            blk.vector(mk("vector"))
            blk.scalar(mk("scalar"))
            blk.sync(mk("sync"))


def _consts():
    eye = np.eye(128, dtype=np.float32)
    uo = np.zeros((128, 64), np.float32)
    u1 = np.triu(np.ones((64, 64), np.float32))
    uo[0:64] = u1
    uo[64:128] = u1
    mL = np.zeros((128, 128), np.float32)
    mG = np.zeros((128, 128), np.float32)
    for q in range(2):
        s = 64 * q
        mL[s:s+64, s:s+64] = np.triu(np.ones((64, 64), np.float32), 1)
        mG[s:s+64, s:s+64] = np.triu(np.ones((64, 64), np.float32))
    hm = np.zeros((128, 2), np.float32)
    hm[0:64, 0] = 1.0
    hm[64:128, 1] = 1.0
    hb = np.zeros((2, 128), np.float32)
    hb[0, 0:64] = 1.0
    hb[1, 64:128] = 1.0
    onesc = np.ones((128, 1), np.float32)
    onesr = np.ones((1, 128), np.float32)
    return dict(eye=eye, uo=uo, mL=mL, mG=mG, hm=hm, hb=hb, onesc=onesc, onesr=onesr)


def build_A():
    nc = bass.Bass("TRN2", target_bir_lowering=False)
    xs = nc.dram_tensor("xs", [SLAB + PT, D], F32, kind="ExternalInput").ap()
    nw = nc.dram_tensor("nw", [128, 8], F32, kind="ExternalInput").ap()
    cw = nc.dram_tensor("cw", [8, 128, 4], F32, kind="ExternalInput").ap()
    cb = nc.dram_tensor("cb", [128, 8], F32, kind="ExternalInput").ap()
    qw = nc.dram_tensor("qw", [D, D], F32, kind="ExternalInput").ap()
    kw = nc.dram_tensor("kw", [D, D], F32, kind="ExternalInput").ap()
    vw = nc.dram_tensor("vw", [D, D], F32, kind="ExternalInput").ap()
    aw = nc.dram_tensor("aw", [D, D], F32, kind="ExternalInput").ap()
    ab = nc.dram_tensor("ab", [128, 8], F32, kind="ExternalInput").ap()
    bw = nc.dram_tensor("bw", [D, H], F32, kind="ExternalInput").ap()
    bb = nc.dram_tensor("bb", [16, 1], F32, kind="ExternalInput").ap()
    onesc = nc.dram_tensor("onesc", [128, 1], F32, kind="ExternalInput").ap()
    onesr = nc.dram_tensor("onesr", [1, 128], F32, kind="ExternalInput").ap()
    eyed = nc.dram_tensor("eye", [128, 128], F32, kind="ExternalInput").ap()
    hmv = nc.dram_tensor("hm", [128, 2], F32, kind="ExternalInput").ap()
    hbv = nc.dram_tensor("hb", [2, 128], F32, kind="ExternalInput").ap()
    qfm = nc.dram_tensor("qfm", [128, 8, PT], F32, kind="ExternalOutput").ap()
    kfm = nc.dram_tensor("kfm", [128, 8, PT], F32, kind="ExternalOutput").ap()
    vfm = nc.dram_tensor("vfm", [128, 8, PT], F32, kind="ExternalOutput").ap()
    lafm = nc.dram_tensor("lafm", [128, 8, PT], F32, kind="ExternalOutput").ap()
    bsfm = nc.dram_tensor("bsfm", [16, PT], F32, kind="ExternalOutput").ap()
    lnxfm = nc.dram_tensor("lnxfm", [128, 8, PT], F32, kind="ExternalOutput").ap()

    p = Prog(nc)
    with ExitStack() as st:
        ec = st.enter_context
        xfm = ec(nc.sbuf_tensor("xfm", [128, 8, SLAB], F32)).ap()
        lnx = xfm
        h0e = ec(nc.sbuf_tensor("h0e", [128, 8, SLAB + 3], F32)).ap()
        hc = ec(nc.sbuf_tensor("hc", [128, 8, SLAB], F32)).ap()
        sq = ec(nc.sbuf_tensor("sq", [128, SLAB], F32)).ap()
        tmp = sq
        bm = ec(nc.sbuf_tensor("bm", [128, SLAB], F32)).ap()
        brs = ec(nc.sbuf_tensor("brs", [128, SLAB], F32)).ap()
        qws = ec(nc.sbuf_tensor("qws", [128, 8, D], F32)).ap()
        kws = ec(nc.sbuf_tensor("kws", [128, 8, D], F32)).ap()
        vws = ec(nc.sbuf_tensor("vws", [128, 8, D], F32)).ap()
        aws = ec(nc.sbuf_tensor("aws", [128, 8, D], F32)).ap()
        bws = ec(nc.sbuf_tensor("bws", [128, 8, H], F32)).ap()
        nws = ec(nc.sbuf_tensor("nws", [128, 8], F32)).ap()
        cbs = ec(nc.sbuf_tensor("cbs", [128, 8], F32)).ap()
        abs_ = ec(nc.sbuf_tensor("abs", [128, 8], F32)).ap()
        cws = ec(nc.sbuf_tensor("cws", [128, 8, 4], F32)).ap()
        bbs = ec(nc.sbuf_tensor("bbs", [16, SLAB], F32)).ap()
        bbcol = ec(nc.sbuf_tensor("bbcol", [16, 1], F32)).ap()
        oc = ec(nc.sbuf_tensor("oc", [128, 1], F32)).ap()
        orow = ec(nc.sbuf_tensor("orow", [1, 128], F32)).ap()
        eyes = ec(nc.sbuf_tensor("eyes", [128, 128], F32)).ap()
        xtm = ec(nc.sbuf_tensor("xtm", [128, D], F32)).ap()
        hms = ec(nc.sbuf_tensor("hms", [128, 2], F32)).ap()
        hbs = ec(nc.sbuf_tensor("hbs", [2, 128], F32)).ap()
        m1 = ec(nc.sbuf_tensor("m1", [1, SLAB], F32)).ap()
        m2 = ec(nc.sbuf_tensor("m2", [1, SLAB], F32)).ap()
        m3 = ec(nc.sbuf_tensor("m3", [1, SLAB], F32)).ap()
        m4 = ec(nc.sbuf_tensor("m4", [1, SLAB], F32)).ap()
        n1 = ec(nc.sbuf_tensor("n1", [2, SLAB], F32)).ap()
        n2 = ec(nc.sbuf_tensor("n2", [2, SLAB], F32)).ap()
        qn = ec(nc.sbuf_tensor("qn", [128, SLAB], F32)).ap()
        pS = ec(nc.psum_tensor("pS", [1, SLAB], F32)).ap()
        pB = ec(nc.psum_tensor("pB", [128, SLAB], F32)).ap()
        pM = ec(nc.psum_tensor("pM", [128, SLAB], F32)).ap()
        pN = ec(nc.psum_tensor("pN", [2, SLAB], F32)).ap()
        pBt = ec(nc.psum_tensor("pBt", [16, SLAB], F32)).ap()

        for ci in range(8):
            p.dma(qws[:, ci, :], qw[128*ci:128*ci+128, :])
            p.dma(kws[:, ci, :], kw[128*ci:128*ci+128, :])
            p.dma(vws[:, ci, :], vw[128*ci:128*ci+128, :])
            p.dma(aws[:, ci, :], aw[128*ci:128*ci+128, :])
            p.dma(bws[:, ci, :], bw[128*ci:128*ci+128, :])
            p.dma(cws[:, ci, :], cw[ci, :, :])
        p.dma(nws, nw)
        p.dma(cbs, cb)
        p.dma(abs_, ab)
        p.dma(bbcol, bb)
        p.dma(oc, onesc)
        p.dma(orow, onesr)
        p.dma(eyes, eyed)
        p.dma(hms, hmv)
        p.dma(hbs, hbv)

        def stats_inv_sqrt(src3, eps):
            for c in range(8):
                p.mm(pS, oc, src3[:, c, :], st=(c == 0), sp=(c == 7))
            p.ts(m1, pS, 1.0/1024.0, OPT.mult)
            for c in range(8):
                p.act(sq, src3[:, c, :], AF.Square)
                p.mm(pS, oc, sq, st=(c == 0), sp=(c == 7))
            p.ts(m2, pS, 1.0/1024.0, OPT.mult)
            return m1, m2

        for sl in range(NSLAB + 1):
            t0 = sl * SLAB
            to = t0 - SLAB
            for i in range(4):
                p.dma(xtm, xs[t0+128*i:t0+128*i+128, :])
                for c in range(8):
                    p.tr(pB[:, 0:128], xtm[:, 128*c:128*c+128], eyes)
                    p.cp(xfm[:, c, 128*i:128*i+128], pB[:, 0:128])
            m, ms2 = stats_inv_sqrt(xfm, LNEPS)
            p.act(m3, m, AF.Square)
            p.tt(m4, ms2, m3, OPT.subtract)
            p.ts(m4, m4, LNEPS, OPT.add)
            p.act(m3, m4, AF.Sqrt)
            p.recip(m4, m3)
            p.mm(pB, orow, m)
            p.cp(bm, pB)
            p.mm(pB, orow, m4)
            p.cp(brs, pB)
            for c in range(8):
                p.tt(lnx[:, c, :], xfm[:, c, :], bm, OPT.subtract)
                p.tt(lnx[:, c, :], lnx[:, c, :], brs, OPT.mult)
            if sl > 0:
                p.dma(lnxfm[:, :, to:to+SLAB], lnx)
            for c in range(8):
                p.act(sq, lnx[:, c, :], AF.Square)
                p.mm(pS, oc, sq, st=(c == 0), sp=(c == 7))
            p.ts(m2, pS, 1.0/1024.0, OPT.mult, EPS, OPT.add)
            p.act(m3, m2, AF.Sqrt)
            p.recip(m4, m3)
            p.mm(pB, orow, m4)
            p.cp(brs, pB)
            for c in range(8):
                p.tt(h0e[:, c, 3:3+SLAB], lnx[:, c, :], brs, OPT.mult)
                p.ts(h0e[:, c, 3:3+SLAB], h0e[:, c, 3:3+SLAB], nws[:, c:c+1], OPT.mult)
            if sl > 0:
                for c in range(8):
                    p.ts(hc[:, c, :], h0e[:, c, 3:3+SLAB], cws[:, c, 3:4], OPT.mult)
                    for tau in range(3):
                        p.ts(tmp, h0e[:, c, tau:tau+SLAB], cws[:, c, tau:tau+1], OPT.mult)
                        p.tt(hc[:, c, :], hc[:, c, :], tmp, OPT.add)
                    p.ts(hc[:, c, :], hc[:, c, :], cbs[:, c:c+1], OPT.add)
                    p.act(hc[:, c, :], hc[:, c, :], AF.Silu)
                for (wsb, dst, kind) in ((qws, qfm, "qk"), (kws, kfm, "qk"), (vws, vfm, "v"), (aws, lafm, "a")):
                    for co in range(8):
                        for ci in range(8):
                            p.mm(pM, wsb[:, ci, 128*co:128*co+128], hc[:, ci, :], st=(ci == 0), sp=(ci == 7))
                        if kind == "qk":
                            p.act(sq, pM, AF.Square)
                            p.mm(pN, hms, sq)
                            p.act(n1, pN, AF.Sqrt)
                            p.ts(n2, n1, EPS, OPT.add)
                            p.recip(n1, n2)
                            p.mm(pB, hbs, n1)
                            p.cp(brs, pB)
                            p.tt(qn, pM, brs, OPT.mult)
                            p.dma(dst[:, co, to:to+SLAB], qn)
                        elif kind == "v":
                            p.cp(qn, pM)
                            p.dma(dst[:, co, to:to+SLAB], qn)
                        else:
                            p.ts(tmp, pM, abs_[:, co:co+1], OPT.add)
                            p.act(qn, tmp, AF.Sigmoid)
                            p.act(qn, qn, AF.Ln)
                            p.dma(dst[:, co, to:to+SLAB], qn)
                for ci in range(8):
                    p.mm(pBt, bws[:, ci, :], hc[:, ci, :], st=(ci == 0), sp=(ci == 7))
                p.ts(bbs, pBt, bbcol, OPT.add)
                p.act(bbs, bbs, AF.Sigmoid)
                p.dma(bsfm[:, to:to+SLAB], bbs)
            for c in range(8):
                p.cp(h0e[:, c, 0:3], h0e[:, c, SLAB:SLAB+3])
        p.seal()
    return nc


def build_B():
    nc = bass.Bass("TRN2", target_bir_lowering=False)
    qd = nc.dram_tensor("qd", [4, 64, PT], F32, kind="ExternalInput").ap()
    kd = nc.dram_tensor("kd", [4, 64, PT], F32, kind="ExternalInput").ap()
    vt = nc.dram_tensor("vt", [4, PT, 64], F32, kind="ExternalInput").ap()
    lat = nc.dram_tensor("lat", [4, PT, 64], F32, kind="ExternalInput").ap()
    bt3 = nc.dram_tensor("bt3", [4, PT, 1], F32, kind="ExternalInput").ap()
    sin = nc.dram_tensor("sin", [4, 64, 64], F32, kind="ExternalInput").ap()
    hnr = nc.dram_tensor("hnr", [4, 64], F32, kind="ExternalInput").ap()
    eye = nc.dram_tensor("eye", [128, 128], F32, kind="ExternalInput").ap()
    uo = nc.dram_tensor("uo", [128, 64], F32, kind="ExternalInput").ap()
    mL = nc.dram_tensor("mL", [128, 128], F32, kind="ExternalInput").ap()
    mG = nc.dram_tensor("mG", [128, 128], F32, kind="ExternalInput").ap()
    onesr = nc.dram_tensor("onesr", [1, 128], F32, kind="ExternalInput").ap()
    oo = nc.dram_tensor("oo", [4, PT, 64], F32, kind="ExternalOutput").ap()
    sout = nc.dram_tensor("sout", [4, 64, 64], F32, kind="ExternalOutput").ap()

    p = Prog(nc)
    with ExitStack() as st:
        ec = st.enter_context
        eyes = ec(nc.sbuf_tensor("eyes", [128, 128], F32)).ap()
        uos = ec(nc.sbuf_tensor("uos", [128, 64], F32)).ap()
        mLs = ec(nc.sbuf_tensor("mLs", [128, 128], F32)).ap()
        mGs = ec(nc.sbuf_tensor("mGs", [128, 128], F32)).ap()
        ors = ec(nc.sbuf_tensor("ors", [1, 128], F32)).ap()
        hr = ec(nc.sbuf_tensor("hr", [1, 64], F32)).ap()
        hnb = ec(nc.sbuf_tensor("hnb", [128, 4, 64], F32)).ap()
        qs = ec(nc.sbuf_tensor("qs", [64, 128], F32)).ap()
        ks = ec(nc.sbuf_tensor("ks", [64, 128], F32)).ap()
        vs = ec(nc.sbuf_tensor("vs", [128, 64], F32)).ap()
        las = ec(nc.sbuf_tensor("las", [128, 64], F32)).ap()
        bcol = ec(nc.sbuf_tensor("bcol", [128, 1], F32)).ap()
        P = ec(nc.sbuf_tensor("P", [64, 128], F32)).ap()
        rP = ec(nc.sbuf_tensor("rP", [64, 128], F32)).ap()
        Kp = ec(nc.sbuf_tensor("Kp", [64, 128], F32)).ap()
        Kd = ec(nc.sbuf_tensor("Kd", [64, 128], F32)).ap()
        Qp = ec(nc.sbuf_tensor("Qp", [64, 128], F32)).ap()
        KpT = ec(nc.sbuf_tensor("KpT", [128, 64], F32)).ap()
        KdT = ec(nc.sbuf_tensor("KdT", [128, 64], F32)).ap()
        LT = ec(nc.sbuf_tensor("LT", [128, 128], F32)).ap()
        GT = ec(nc.sbuf_tensor("GT", [128, 128], F32)).ap()
        LTn = ec(nc.sbuf_tensor("LTn", [128, 128], F32)).ap()
        Ltr = ec(nc.sbuf_tensor("Ltr", [128, 128], F32)).ap()
        Lp = [ec(nc.sbuf_tensor("Lp%d" % i, [128, 128], F32)).ap() for i in range(5)]
        Xa = ec(nc.sbuf_tensor("Xa", [128, 128], F32)).ap()
        Xb = ec(nc.sbuf_tensor("Xb", [128, 128], F32)).ap()
        CT = ec(nc.sbuf_tensor("CT", [64, 128], F32)).ap()
        Wt = ec(nc.sbuf_tensor("Wt", [128, 64], F32)).ap()
        WB = ec(nc.sbuf_tensor("WB", [128, 64], F32)).ap()
        Ssb = [ec(nc.sbuf_tensor("S%d" % pp, [64, 2, 64], F32)).ap() for pp in range(4)]
        Osq = ec(nc.sbuf_tensor("Osq", [128, 64], F32)).ap()
        Ot = ec(nc.sbuf_tensor("Ot", [128, 64], F32)).ap()
        ms1 = ec(nc.sbuf_tensor("ms1", [128, 1], F32)).ap()
        ms2 = ec(nc.sbuf_tensor("ms2", [128, 1], F32)).ap()
        pP = ec(nc.psum_tensor("pP", [64, 128], F32)).ap()
        pM1 = ec(nc.psum_tensor("pM1", [128, 128], F32)).ap()
        pW = ec(nc.psum_tensor("pW", [128, 64], F32)).ap()
        pO = ec(nc.psum_tensor("pO", [128, 64], F32)).ap()
        pS6 = ec(nc.psum_tensor("pS6", [64, 64], F32)).ap()

        p.dma(eyes, eye)
        p.dma(uos, uo)
        p.dma(mLs, mL)
        p.dma(mGs, mG)
        p.dma(ors, onesr)
        for pp in range(4):
            p.dma(Ssb[pp][:, 0, :], sin[pp, :, :])
            p.dma(hr, hnr[pp:pp+1, :])
            p.mm(pW, ors, hr)
            p.cp(hnb[:, pp, :], pW)

        for pp in range(4):
            par = 0
            for tile in range(PT // 128):
                ts0 = tile * 128
                p.dma(qs, qd[pp, :, ts0:ts0+128])
                p.dma(ks, kd[pp, :, ts0:ts0+128])
                p.dma(vs, vt[pp, ts0:ts0+128, :])
                p.dma(las, lat[pp, ts0:ts0+128, :])
                p.dma(bcol, bt3[pp, ts0:ts0+128, :])
                for q in range(2):
                    p.mm(pP[:, 64*q:64*q+64], las[64*q:64*q+64, :], uos[64*q:64*q+64, :])
                p.act(P, pP, AF.Exp)
                p.recip(rP, P)
                p.tt(Kp, ks, P, OPT.mult)
                p.tt(Kd, ks, rP, OPT.mult)
                p.tt(Qp, qs, P, OPT.mult)
                p.tr(pM1[:, 0:64], Kp, eyes[0:64, 0:64])
                p.cp(KpT, pM1[:, 0:64])
                p.tr(pM1[:, 0:64], Kd, eyes[0:64, 0:64])
                p.cp(KdT, pM1[:, 0:64])
                for q in range(2):
                    p.mm(pM1[64*q:64*q+64, 64*q:64*q+64], Kd[:, 64*q:64*q+64], Kp[:, 64*q:64*q+64])
                p.tt(LT, pM1, mLs, OPT.mult)
                p.ts(LT, LT, bcol, OPT.mult)
                for q in range(2):
                    p.mm(pM1[64*q:64*q+64, 64*q:64*q+64], Kd[:, 64*q:64*q+64], Qp[:, 64*q:64*q+64])
                p.tt(GT, pM1, mGs, OPT.mult)
                p.ts(LTn, LT, -1.0, OPT.mult)
                prev = LT
                for i in range(5):
                    p.tr(pM1, prev, eyes)
                    p.cp(Ltr, pM1)
                    p.mm(pM1, Ltr, prev)
                    p.cp(Lp[i], pM1)
                    prev = Lp[i]
                p.cp(Xa[:, 0:64], vs)
                p.cp(Xa[:, 64:128], KpT)
                cur, nxt = Xa, Xb
                for Fm in (Lp[4], Lp[3], Lp[2], Lp[1], Lp[0], LTn):
                    p.mm(pM1, Fm, cur)
                    p.tt(nxt, cur, pM1, OPT.add)
                    cur, nxt = nxt, cur
                p.tr(pP, cur[:, 64:128], eyes)
                p.cp(CT, pP)
                for q in range(2):
                    S0 = Ssb[pp][:, par, :]
                    S1 = Ssb[pp][:, 1 - par, :]
                    p.mm(pW[64*q:64*q+64, :], CT[:, 64*q:64*q+64], S0)
                    p.tt(Wt[64*q:64*q+64, :], cur[64*q:64*q+64, 0:64], pW[64*q:64*q+64, :], OPT.subtract)
                    p.ts(WB[64*q:64*q+64, :], Wt[64*q:64*q+64, :], bcol[64*q:64*q+64, :], OPT.mult)
                    p.mm(pO[64*q:64*q+64, :], Qp[:, 64*q:64*q+64], S0, st=True, sp=False)
                    p.mm(pO[64*q:64*q+64, :], GT[64*q:64*q+64, 64*q:64*q+64], WB[64*q:64*q+64, :], st=False, sp=True)
                    p.mm(pS6, KdT[64*q:64*q+64, :], WB[64*q:64*q+64, :])
                    p.tt(S1, S0, pS6, OPT.add)
                    p.ts(S1, S1, P[:, 64*q+63:64*q+64], OPT.mult)
                    par = 1 - par
                p.act(Osq, pO, AF.Square)
                p.red(ms1, Osq, OPT.add)
                p.ts(ms1, ms1, 1.0/64.0, OPT.mult, EPS, OPT.add)
                p.act(ms2, ms1, AF.Sqrt)
                p.recip(ms1, ms2)
                p.ts(Ot, pO, ms1, OPT.mult)
                p.tt(Ot, Ot, hnb[:, pp, :], OPT.mult)
                p.dma(oo[pp, ts0:ts0+128, :], Ot)
            p.dma(sout[pp, :, :], Ssb[pp][:, par, :])
        p.seal()
    return nc


def build_C1():
    nc = bass.Bass("TRN2", target_bir_lowering=False)
    lnf = nc.dram_tensor("lnf", [128, 8, PT], F32, kind="ExternalInput").ap()
    lnt = nc.dram_tensor("lnt", [PT, D], F32, kind="ExternalInput").ap()
    ofm = nc.dram_tensor("ofm", [128, 8, PT], F32, kind="ExternalInput").ap()
    ow = nc.dram_tensor("ow", [D, D], F32, kind="ExternalInput").ap()
    u1w = nc.dram_tensor("u1w", [D, 64], F32, kind="ExternalInput").ap()
    u1b = nc.dram_tensor("u1b", [1, 64], F32, kind="ExternalInput").ap()
    u2w = nc.dram_tensor("u2w", [64, D], F32, kind="ExternalInput").ap()
    u2b = nc.dram_tensor("u2b", [1, D], F32, kind="ExternalInput").ap()
    fnw = nc.dram_tensor("fnw", [1, D], F32, kind="ExternalInput").ap()
    eye = nc.dram_tensor("eye", [128, 128], F32, kind="ExternalInput").ap()
    onesr = nc.dram_tensor("onesr", [1, 128], F32, kind="ExternalInput").ap()
    ytm = nc.dram_tensor("ytm", [PT, D], F32, kind="ExternalOutput").ap()
    y2tm = nc.dram_tensor("y2tm", [PT, D], F32, kind="ExternalOutput").ap()

    p = Prog(nc)
    with ExitStack() as st:
        ec = st.enter_context
        ows = ec(nc.sbuf_tensor("ows", [128, 8, D], F32)).ap()
        u1s = ec(nc.sbuf_tensor("u1s", [128, 8, 64], F32)).ap()
        u2s = ec(nc.sbuf_tensor("u2s", [64, D], F32)).ap()
        eyes = ec(nc.sbuf_tensor("eyes", [128, 128], F32)).ap()
        ors = ec(nc.sbuf_tensor("ors", [1, 128], F32)).ap()
        row = ec(nc.sbuf_tensor("row", [1, D], F32)).ap()
        b1b = ec(nc.sbuf_tensor("b1b", [128, 64], F32)).ap()
        b2b = ec(nc.sbuf_tensor("b2b", [128, D], F32)).ap()
        fnb = ec(nc.sbuf_tensor("fnb", [128, D], F32)).ap()
        lfs = ec(nc.sbuf_tensor("lfs", [128, 8, 128], F32)).ap()
        ofs = ec(nc.sbuf_tensor("ofs", [128, 8, 128], F32)).ap()
        ltt = ec(nc.sbuf_tensor("ltt", [128, D], F32)).ap()
        s1 = ec(nc.sbuf_tensor("s1", [128, 64], F32)).ap()
        s1f = ec(nc.sbuf_tensor("s1f", [64, 128], F32)).ap()
        gate = ec(nc.sbuf_tensor("gate", [128, D], F32)).ap()
        yt = ec(nc.sbuf_tensor("yt", [128, D], F32)).ap()
        y2 = ec(nc.sbuf_tensor("y2", [128, D], F32)).ap()
        sq = ec(nc.sbuf_tensor("sq", [128, D], F32)).ap()
        c1 = ec(nc.sbuf_tensor("c1", [128, 1], F32)).ap()
        c2 = ec(nc.sbuf_tensor("c2", [128, 1], F32)).ap()
        c3 = ec(nc.sbuf_tensor("c3", [128, 1], F32)).ap()
        pA = ec(nc.psum_tensor("pA", [128, 512], F32)).ap()
        pA2 = ec(nc.psum_tensor("pA2", [128, 512], F32)).ap()
        pU = ec(nc.psum_tensor("pU", [128, 64], F32)).ap()
        pT1 = ec(nc.psum_tensor("pT1", [64, 128], F32)).ap()
        pB1 = ec(nc.psum_tensor("pB1", [128, 512], F32)).ap()

        for ci in range(8):
            p.dma(ows[:, ci, :], ow[128*ci:128*ci+128, :])
            p.dma(u1s[:, ci, :], u1w[128*ci:128*ci+128, :])
        p.dma(u2s, u2w)
        p.dma(eyes, eye)
        p.dma(ors, onesr)
        p.dma(row[:, 0:64], u1b)
        p.mm(pU, ors, row[:, 0:64])
        p.cp(b1b, pU)
        p.dma(row, u2b)
        for h in range(2):
            p.mm(pB1, ors, row[:, 512*h:512*h+512])
            p.cp(b2b[:, 512*h:512*h+512], pB1)
        p.dma(row, fnw)
        for h in range(2):
            p.mm(pB1, ors, row[:, 512*h:512*h+512])
            p.cp(fnb[:, 512*h:512*h+512], pB1)

        for sub in range(PT // 128):
            t0 = sub * 128
            p.dma(lfs, lnf[:, :, t0:t0+128])
            p.dma(ofs, ofm[:, :, t0:t0+128])
            p.dma(ltt, lnt[t0:t0+128, :])
            for h in range(2):
                pp = pA if h == 0 else pA2
                for k in range(8):
                    p.mm(pp, ofs[:, k, :], ows[:, k, 512*h:512*h+512], st=(k == 0), sp=(k == 7))
            for k in range(8):
                p.mm(pU, lfs[:, k, :], u1s[:, k, :], st=(k == 0), sp=(k == 7))
            p.tt(s1, pU, b1b, OPT.add)
            p.act(s1, s1, AF.Silu)
            p.tr(pT1, s1, eyes)
            p.cp(s1f, pT1)
            for h in range(2):
                pp = pB1
                p.mm(pp, s1f, u2s[:, 512*h:512*h+512])
                p.tt(gate[:, 512*h:512*h+512], pp, b2b[:, 512*h:512*h+512], OPT.add)
            p.act(gate, gate, AF.Sigmoid)
            p.tt(yt[:, 0:512], pA, gate[:, 0:512], OPT.mult)
            p.tt(yt[:, 512:1024], pA2, gate[:, 512:1024], OPT.mult)
            p.tt(yt, yt, ltt, OPT.add)
            p.dma(ytm[t0:t0+128, :], yt)
            p.act(sq, yt, AF.Square)
            p.red(c1, sq, OPT.add)
            p.ts(c1, c1, 1.0/1024.0, OPT.mult, EPS, OPT.add)
            p.act(c2, c1, AF.Sqrt)
            p.recip(c3, c2)
            p.ts(y2, yt, c3, OPT.mult)
            p.tt(y2, y2, fnb, OPT.mult)
            p.dma(y2tm[t0:t0+128, :], y2)
        p.seal()
    return nc


def build_C23(third):
    nc = bass.Bass("TRN2", target_bir_lowering=False)
    y2f = nc.dram_tensor("y2f", [128, 8, PT], F32, kind="ExternalInput").ap()
    w = nc.dram_tensor("w", [D, 4096], F32, kind="ExternalInput").ap()
    if third:
        g1 = nc.dram_tensor("g1", [128, 32, PT], F32, kind="ExternalInput").ap()
    go = nc.dram_tensor("go", [128, 32, PT], F32, kind="ExternalOutput").ap()
    p = Prog(nc)
    with ExitStack() as st:
        ec = st.enter_context
        ws = ec(nc.sbuf_tensor("ws", [128, 8, 4096], F32)).ap()
        y2s = ec(nc.sbuf_tensor("y2s", [128, 8, SLAB], F32)).ap()
        g1s = ec(nc.sbuf_tensor("g1s", [128, SLAB], F32)).ap()
        res = ec(nc.sbuf_tensor("res", [128, SLAB], F32)).ap()
        pM = ec(nc.psum_tensor("pM", [128, SLAB], F32)).ap()
        for ci in range(8):
            p.dma(ws[:, ci, :], w[128*ci:128*ci+128, :])
        for sl in range(NSLAB):
            t0 = sl * SLAB
            p.dma(y2s, y2f[:, :, t0:t0+SLAB])
            for of in range(32):
                for ci in range(8):
                    p.mm(pM, ws[:, ci, 128*of:128*of+128], y2s[:, ci, :], st=(ci == 0), sp=(ci == 7))
                if third:
                    p.dma(g1s, g1[:, of, t0:t0+SLAB])
                    p.tt(res, pM, g1s, OPT.mult)
                else:
                    p.act(res, pM, AF.Silu)
                p.dma(go[:, of, t0:t0+SLAB], res)
        p.seal()
    return nc


def build_C4():
    nc = bass.Bass("TRN2", target_bir_lowering=False)
    gf = nc.dram_tensor("gf", [128, 32, PT], F32, kind="ExternalInput").ap()
    w2 = nc.dram_tensor("w2", [4096, D], F32, kind="ExternalInput").ap()
    xin = nc.dram_tensor("xin", [PT, D], F32, kind="ExternalInput").ap()
    ytm = nc.dram_tensor("ytm", [PT, D], F32, kind="ExternalInput").ap()
    out1 = nc.dram_tensor("out1", [PT, D], F32, kind="ExternalOutput").ap()
    xq = nc.dram_tensor("xq", [PT, D], F32, kind="ExternalOutput").ap()
    p = Prog(nc)
    with ExitStack() as st:
        ec = st.enter_context
        w2s = ec(nc.sbuf_tensor("w2s", [128, 32, D], F32)).ap()
        gs = ec(nc.sbuf_tensor("gs", [128, 32, 128], F32)).ap()
        yts = ec(nc.sbuf_tensor("yts", [128, D], F32)).ap()
        xs = ec(nc.sbuf_tensor("xs", [128, D], F32)).ap()
        o1 = ec(nc.sbuf_tensor("o1", [128, D], F32)).ap()
        ln2 = ec(nc.sbuf_tensor("ln2", [128, D], F32)).ap()
        sq = ec(nc.sbuf_tensor("sq", [128, D], F32)).ap()
        xqs = ec(nc.sbuf_tensor("xqs", [128, D], F32)).ap()
        xi = ec(nc.sbuf_tensor("xi", [128, D], I32)).ap()
        c1 = ec(nc.sbuf_tensor("c1", [128, 1], F32)).ap()
        c2 = ec(nc.sbuf_tensor("c2", [128, 1], F32)).ap()
        c3 = ec(nc.sbuf_tensor("c3", [128, 1], F32)).ap()
        c4 = ec(nc.sbuf_tensor("c4", [128, 1], F32)).ap()
        pY = ec(nc.psum_tensor("pY", [128, 512], F32)).ap()
        pY2 = ec(nc.psum_tensor("pY2", [128, 512], F32)).ap()
        for ci in range(32):
            p.dma(w2s[:, ci, :], w2[128*ci:128*ci+128, :])
        for sub in range(PT // 128):
            t0 = sub * 128
            p.dma(gs, gf[:, :, t0:t0+128])
            p.dma(yts, ytm[t0:t0+128, :])
            p.dma(xs, xin[t0:t0+128, :])
            for h in range(2):
                pp = pY if h == 0 else pY2
                for k in range(32):
                    p.mm(pp, gs[:, k, :], w2s[:, k, 512*h:512*h+512], st=(k == 0), sp=(k == 31))
            p.tt(o1[:, 0:512], pY, yts[:, 0:512], OPT.add)
            p.tt(o1[:, 512:1024], pY2, yts[:, 512:1024], OPT.add)
            p.tt(o1, o1, xs, OPT.add)
            p.dma(out1[t0:t0+128, :], o1)
            p.red(c1, o1, OPT.add)
            p.ts(c1, c1, 1.0/1024.0, OPT.mult)
            p.act(sq, o1, AF.Square)
            p.red(c2, sq, OPT.add)
            p.ts(c2, c2, 1.0/1024.0, OPT.mult)
            p.act(c3, c1, AF.Square)
            p.tt(c2, c2, c3, OPT.subtract)
            p.ts(c2, c2, LNEPS, OPT.add)
            p.act(c3, c2, AF.Sqrt)
            p.recip(c4, c3)
            p.ts(ln2, o1, c1, OPT.subtract)
            p.ts(ln2, ln2, c4, OPT.mult)
            p.red(c1, ln2, OPT.max, absv=True)
            p.ts(c1, c1, 1e-5, OPT.max)
            p.recip(c2, c1)
            p.ts(c2, c2, 127.0, OPT.mult)
            p.ts(c3, c1, 1.0/127.0, OPT.mult)
            p.ts(xqs, ln2, c2, OPT.mult)
            p.cp(xi, xqs)
            p.cp(xqs, xi)
            p.ts(xqs, xqs, 127.0, OPT.min)
            p.ts(xqs, xqs, -128.0, OPT.max)
            p.ts(xqs, xqs, c3, OPT.mult)
            p.dma(xq[t0:t0+128, :], xqs)
        p.seal()
    return nc


def build_D0():
    nc = bass.Bass("TRN2", target_bir_lowering=False)
    wi = [nc.dram_tensor("w%d" % i, [4096, D], F32, kind="ExternalInput").ap() for i in range(3)]
    onesc = nc.dram_tensor("onesc", [128, 1], F32, kind="ExternalInput").ap()
    onesr = nc.dram_tensor("onesr", [1, 128], F32, kind="ExternalInput").ap()
    wo = [nc.dram_tensor("wq%d" % i, [4096, D], F32, kind="ExternalOutput").ap() for i in range(3)]
    p = Prog(nc)
    with ExitStack() as st:
        ec = st.enter_context
        wt = ec(nc.sbuf_tensor("wt", [128, D], F32)).ap()
        wj = ec(nc.sbuf_tensor("wj", [128, D], I32)).ap()
        acc = ec(nc.sbuf_tensor("acc", [128, 1], F32)).ap()
        pt = ec(nc.sbuf_tensor("pt", [128, 1], F32)).ap()
        tot = ec(nc.sbuf_tensor("tot", [1, 1], F32)).ap()
        mcol = ec(nc.sbuf_tensor("mcol", [128, 1], F32)).ap()
        scol = ec(nc.sbuf_tensor("scol", [128, 1], F32)).ap()
        ocs = ec(nc.sbuf_tensor("ocs", [128, 1], F32)).ap()
        ors = ec(nc.sbuf_tensor("ors", [1, 128], F32)).ap()
        pS = ec(nc.psum_tensor("pS", [1, 1], F32)).ap()
        pC = ec(nc.psum_tensor("pC", [128, 1], F32)).ap()
        p.dma(ocs, onesc)
        p.dma(ors, onesr)
        for i in range(3):
            p.v(lambda e, a=acc: e.memset(a, 0.0))
            for t in range(32):
                p.dma(wt, wi[i][128*t:128*t+128, :])
                p.red(pt, wt, OPT.add, absv=True)
                p.tt(acc, acc, pt, OPT.add)
            p.mm(pS, acc, ocs)
            p.ts(tot, pS, 1.0/(4096.0*1024.0), OPT.mult, 1e-5, OPT.max)
            p.mm(pC, ors, tot)
            p.cp(mcol, pC)
            p.recip(scol, mcol)
            for t in range(32):
                p.dma(wt, wi[i][128*t:128*t+128, :])
                p.ts(wt, wt, scol, OPT.mult)
                p.cp(wj, wt)
                p.cp(wt, wj)
                p.ts(wt, wt, 1.0, OPT.min)
                p.ts(wt, wt, -1.0, OPT.max)
                p.ts(wt, wt, mcol, OPT.mult)
                p.dma(wo[i][128*t:128*t+128, :], wt)
        p.seal()
    return nc


def build_D23(third):
    nc = bass.Bass("TRN2", target_bir_lowering=False)
    xqf = nc.dram_tensor("xqf", [128, 8, PT], F32, kind="ExternalInput").ap()
    wT = nc.dram_tensor("wT", [128, 8, 4096], F32, kind="ExternalInput").ap()
    if third:
        g1 = nc.dram_tensor("g1", [PT, 4096], F32, kind="ExternalInput").ap()
    go = nc.dram_tensor("go", [PT, 4096], F32, kind="ExternalOutput").ap()
    p = Prog(nc)
    with ExitStack() as st:
        ec = st.enter_context
        ws = ec(nc.sbuf_tensor("ws", [128, 8, 4096], F32)).ap()
        xqs = ec(nc.sbuf_tensor("xqs", [128, 8, 128], F32)).ap()
        gt = ec(nc.sbuf_tensor("gt", [128, 4096], F32)).ap()
        g1s = ec(nc.sbuf_tensor("g1s", [128, 4096], F32)).ap()
        gi = ec(nc.sbuf_tensor("gi", [128, 4096], I32)).ap()
        c1 = ec(nc.sbuf_tensor("c1", [128, 1], F32)).ap()
        c2 = ec(nc.sbuf_tensor("c2", [128, 1], F32)).ap()
        c3 = ec(nc.sbuf_tensor("c3", [128, 1], F32)).ap()
        pM = [ec(nc.psum_tensor("pM%d" % h, [128, 512], F32)).ap() for h in range(8)]
        for ci in range(8):
            p.dma(ws[:, ci, :], wT[:, ci, :])
        for sub in range(PT // 128):
            t0 = sub * 128
            p.dma(xqs, xqf[:, :, t0:t0+128])
            for h in range(8):
                for k in range(8):
                    p.mm(pM[h], xqs[:, k, :], ws[:, k, 512*h:512*h+512], st=(k == 0), sp=(k == 7))
            if not third:
                for h in range(8):
                    p.act(gt[:, 512*h:512*h+512], pM[h], AF.Silu)
                p.dma(go[t0:t0+128, :], gt)
            else:
                p.dma(g1s, g1[t0:t0+128, :])
                for h in range(8):
                    p.tt(gt[:, 512*h:512*h+512], pM[h], g1s[:, 512*h:512*h+512], OPT.mult)
                p.red(c1, gt, OPT.max, absv=True)
                p.ts(c1, c1, 1e-5, OPT.max)
                p.recip(c2, c1)
                p.ts(c2, c2, 127.0, OPT.mult)
                p.ts(c3, c1, 1.0/127.0, OPT.mult)
                p.ts(gt, gt, c2, OPT.mult)
                p.cp(gi, gt)
                p.cp(gt, gi)
                p.ts(gt, gt, 127.0, OPT.min)
                p.ts(gt, gt, -128.0, OPT.max)
                p.ts(gt, gt, c3, OPT.mult)
                p.dma(go[t0:t0+128, :], gt)
        p.seal()
    return nc


def build_D4():
    nc = bass.Bass("TRN2", target_bir_lowering=False)
    guf = nc.dram_tensor("guf", [128, 32, PT], F32, kind="ExternalInput").ap()
    wT = nc.dram_tensor("wT", [128, 32, D], F32, kind="ExternalInput").ap()
    o1 = nc.dram_tensor("o1", [PT, D], F32, kind="ExternalInput").ap()
    yout = nc.dram_tensor("yout", [PT, D], F32, kind="ExternalOutput").ap()
    p = Prog(nc)
    with ExitStack() as st:
        ec = st.enter_context
        ws = ec(nc.sbuf_tensor("ws", [128, 32, D], F32)).ap()
        gs = ec(nc.sbuf_tensor("gs", [128, 32, 128], F32)).ap()
        o1s = ec(nc.sbuf_tensor("o1s", [128, D], F32)).ap()
        yo = ec(nc.sbuf_tensor("yo", [128, D], F32)).ap()
        pY = ec(nc.psum_tensor("pY", [128, 512], F32)).ap()
        pY2 = ec(nc.psum_tensor("pY2", [128, 512], F32)).ap()
        for ci in range(32):
            p.dma(ws[:, ci, :], wT[:, ci, :])
        for sub in range(PT // 128):
            t0 = sub * 128
            p.dma(gs, guf[:, :, t0:t0+128])
            p.dma(o1s, o1[t0:t0+128, :])
            for h in range(2):
                pp = pY if h == 0 else pY2
                for k in range(32):
                    p.mm(pp, gs[:, k, :], ws[:, k, 512*h:512*h+512], st=(k == 0), sp=(k == 31))
            p.tt(yo[:, 0:512], pY, o1s[:, 0:512], OPT.add)
            p.tt(yo[:, 512:1024], pY2, o1s[:, 512:1024], OPT.add)
            p.dma(yout[t0:t0+128, :], yo)
        p.seal()
    return nc


def _run(nc, in_maps):
    res = run_bass_kernel_spmd(nc, in_maps, core_ids=list(range(NCORES)))
    return res.results


def kernel(**inp):
    inp = {k: np.asarray(v, dtype=np.float32) for k, v in inp.items()}
    C = _consts()
    xall = inp["L0_attn_x"].reshape(B * T, D)

    ncA = build_A()
    inA = []
    for c in range(NCORES):
        s = c * PT
        if c % 4 == 0:
            pre = np.zeros((SLAB, D), np.float32)
        else:
            pre = xall[s-SLAB:s]
        m = dict(xs=np.ascontiguousarray(np.concatenate([pre, xall[s:s+PT]], 0)),
                 nw=np.ascontiguousarray(inp["L0_attn_norm_in_w"].reshape(8, 128).T),
                 cw=inp["L0_attn_conv_w"].reshape(1024, 4).reshape(8, 128, 4),
                 cb=np.ascontiguousarray(inp["L0_attn_conv_b"].reshape(8, 128).T),
                 qw=inp["L0_attn_q_w"], kw=inp["L0_attn_k_w"], vw=inp["L0_attn_v_w"],
                 aw=inp["L0_attn_alpha_w"], ab=np.ascontiguousarray(inp["L0_attn_alpha_b"].reshape(8, 128).T),
                 bw=inp["L0_attn_beta_w"], bb=inp["L0_attn_beta_b"].reshape(16, 1),
                 onesc=C["onesc"], onesr=C["onesr"], eye=C["eye"], hm=C["hm"], hb=C["hb"])
        inA.append(m)
    rA = _run(ncA, inA)

    def full_fm(name):
        return np.concatenate([rA[c][name] for c in range(NCORES)], axis=2)

    qf, kf, vf, laf, lnf = (full_fm(n) for n in ("qfm", "kfm", "vfm", "lafm", "lnxfm"))
    bsf = np.concatenate([rA[c]["bsfm"] for c in range(NCORES)], axis=1)

    ncB = build_B()
    hnw = inp["L0_attn_head_norm_w"]
    Scur = [np.zeros((4, 64, 64), np.float32) for _ in range(NCORES)]
    o_parts = [[] for _ in range(NCORES)]
    pair_info = []
    for c in range(NCORES):
        bb_ = c // 4
        hs = [4*(c % 4) + j for j in range(4)]
        pair_info.append((bb_, hs))
    for j in range(4):
        w0 = j * PT
        inB = []
        for c in range(NCORES):
            bb_, hs = pair_info[c]
            base = bb_ * T
            qd = np.stack([qf[(h % 2)*64:(h % 2)*64+64, h//2, base+w0:base+w0+PT] for h in hs])
            kdd = np.stack([kf[(h % 2)*64:(h % 2)*64+64, h//2, base+w0:base+w0+PT] for h in hs])
            vt = np.stack([vf[(h % 2)*64:(h % 2)*64+64, h//2, base+w0:base+w0+PT].T for h in hs])
            lat = np.stack([laf[(h % 2)*64:(h % 2)*64+64, h//2, base+w0:base+w0+PT].T for h in hs])
            bt3 = np.stack([bsf[h, base+w0:base+w0+PT][:, None] for h in hs])
            hnr = np.stack([hnw[h] for h in hs])
            inB.append(dict(qd=np.ascontiguousarray(qd), kd=np.ascontiguousarray(kdd),
                            vt=np.ascontiguousarray(vt), lat=np.ascontiguousarray(lat),
                            bt3=np.ascontiguousarray(bt3), sin=Scur[c], hnr=np.ascontiguousarray(hnr),
                            eye=C["eye"], uo=C["uo"], mL=C["mL"], mG=C["mG"], onesr=C["onesr"]))
        rB = _run(ncB, inB)
        for c in range(NCORES):
            o_parts[c].append(rB[c]["oo"])
            Scur[c] = rB[c]["sout"]
    ofull = np.zeros((B, T, H, DV), np.float32)
    for c in range(NCORES):
        bb_, hs = pair_info[c]
        oc = np.concatenate(o_parts[c], axis=1)
        for idx, h in enumerate(hs):
            ofull[bb_, :, h, :] = oc[idx]
    otm = ofull.reshape(B * T, D)

    ncC1 = build_C1()
    inC1 = []
    for c in range(NCORES):
        s = c * PT
        ofm = np.ascontiguousarray(otm[s:s+PT].reshape(PT, 8, 128).transpose(2, 1, 0))
        lnt = np.ascontiguousarray(lnf[:, :, s:s+PT].transpose(2, 1, 0).reshape(PT, D))
        inC1.append(dict(lnf=np.ascontiguousarray(lnf[:, :, s:s+PT]), lnt=lnt, ofm=ofm,
                         ow=inp["L0_attn_out_w"], u1w=inp["L0_attn_u1_w"],
                         u1b=inp["L0_attn_u1_b"].reshape(1, 64), u2w=inp["L0_attn_u2_w"],
                         u2b=inp["L0_attn_u2_b"].reshape(1, D),
                         fnw=inp["L0_attn_ff_norm_w"].reshape(1, D),
                         eye=C["eye"], onesr=C["onesr"]))
    rC1 = _run(ncC1, inC1)

    ncC2 = build_C23(False)
    ncC3 = build_C23(True)
    ncC4 = build_C4()
    inC2, y2fs = [], []
    for c in range(NCORES):
        y2 = rC1[c]["y2tm"]
        y2f = np.ascontiguousarray(y2.reshape(PT, 8, 128).transpose(2, 1, 0))
        y2fs.append(y2f)
        inC2.append(dict(y2f=y2f, w=inp["L0_attn_ff_w1"]))
    rC2 = _run(ncC2, inC2)
    inC3 = [dict(y2f=y2fs[c], w=inp["L0_attn_ff_w3"], g1=rC2[c]["go"]) for c in range(NCORES)]
    rC3 = _run(ncC3, inC3)
    inC4 = []
    for c in range(NCORES):
        s = c * PT
        inC4.append(dict(gf=rC3[c]["go"], w2=inp["L0_attn_ff_w2"],
                         xin=np.ascontiguousarray(xall[s:s+PT]), ytm=rC1[c]["ytm"]))
    rC4 = _run(ncC4, inC4)

    ncD0 = build_D0()
    inD0 = [dict(w0=inp["L0_ffn_w_gate"], w1=inp["L0_ffn_w_up"], w2=inp["L0_ffn_w_down"].T.copy(),
                 onesc=C["onesc"], onesr=C["onesr"]) for _ in range(NCORES)]
    rD0 = _run(ncD0, inD0)
    wgq, wuq, wdq = rD0[0]["wq0"], rD0[0]["wq1"], rD0[0]["wq2"]
    wgT = np.ascontiguousarray(wgq.T.reshape(8, 128, 4096).transpose(1, 0, 2))
    wuT = np.ascontiguousarray(wuq.T.reshape(8, 128, 4096).transpose(1, 0, 2))
    wdT = np.ascontiguousarray(wdq.reshape(32, 128, D).transpose(1, 0, 2))

    ncD2 = build_D23(False)
    ncD3 = build_D23(True)
    ncD4 = build_D4()
    inD2 = []
    xqfs = []
    for c in range(NCORES):
        xq = rC4[c]["xq"]
        xqf = np.ascontiguousarray(xq.reshape(PT, 8, 128).transpose(2, 1, 0))
        xqfs.append(xqf)
        inD2.append(dict(xqf=xqf, wT=wgT))
    rD2 = _run(ncD2, inD2)
    inD3 = [dict(xqf=xqfs[c], wT=wuT, g1=rD2[c]["go"]) for c in range(NCORES)]
    rD3 = _run(ncD3, inD3)
    inD4 = []
    for c in range(NCORES):
        gu = rD3[c]["go"]
        guf = np.ascontiguousarray(gu.reshape(PT, 32, 128).transpose(2, 1, 0))
        inD4.append(dict(guf=guf, wT=wdT, o1=rC4[c]["out1"]))
    rD4 = _run(ncD4, inD4)

    out = np.concatenate([rD4[c]["yout"] for c in range(NCORES)], axis=0)
    return out.reshape(B, T, D)
